# version 06 (final)
# baseline (speedup 1.0000x reference)
import contextlib
import math

import numpy as np
import ml_dtypes

import concourse.bass as bass
import concourse.mybir as mybir
from concourse.bass_utils import run_bass_kernel_spmd

F32 = mybir.dt.float32
BF16 = mybir.dt.bfloat16
FP8 = mybir.dt.float8e4
I32 = mybir.dt.int32
ALU = mybir.AluOpType
ACTF = mybir.ActivationFunctionType
AX = mybir.AxisListType



class TT:
    __slots__ = ("h", "name", "writers", "readers")

    def __init__(self, h, name=""):
        self.h = h
        self.name = name
        self.writers = {}
        self.readers = {}

    def __getitem__(self, key):
        return self.h.__getitem__(key)

    @property
    def ap(self):
        nd = len(self.h.shape)
        return self.h.__getitem__(tuple(slice(None) for _ in range(nd)))


class Prog:
    ENGINES = ("sync", "vector", "scalar", "tensor", "gpsimd")

    def __init__(self, nc, es):
        self.nc = nc
        self.es = es
        self.sem = {}
        self.cnt = {}
        for s in self.ENGINES + ("cc",):
            self.sem[s] = es.enter_context(nc.semaphore(f"sem_{s}"))
            self.cnt[s] = 0
        self.water = {e: {} for e in self.ENGINES}
        self.records = {e: [] for e in self.ENGINES}
        self.pending = {e: [] for e in self.ENGINES}

    def channel(self, name):
        key = "dch_" + name
        if key not in self.sem:
            self.sem[key] = self.es.enter_context(self.nc.semaphore(key))
            self.cnt[key] = 0
        return key

    def sb(self, name, shape, dtype, scope=None):
        h = (scope or self.es).enter_context(
            self.nc.sbuf_tensor(name, shape, dtype))
        return TT(h, name)

    def ps(self, name, shape, dtype=F32, scope=None):
        h = (scope or self.es).enter_context(
            self.nc.psum_tensor(name, shape, dtype))
        return TT(h, name)

    def dram(self, name, shape, dtype, kind=None):
        if kind is None:
            h = self.nc.dram_tensor(name, shape, dtype)
        else:
            h = self.nc.dram_tensor(name, shape, dtype, kind=kind)
        return TT(h.ap(), name)

    def emit(self, eng, fn, outs=(), ins=(), sem=None, inc=1, track=True,
             tag=None):
        need = {}

        def want(d):
            for s, v in d.items():
                if self.water[eng].get(s, 0) < v and need.get(s, 0) < v:
                    need[s] = v

        for t in ins:
            want(t.writers)
        for t in outs:
            want(t.writers)
            want(t.readers)
        for s, v in need.items():
            self.water[eng][s] = v
        waits = list(need.items())
        if track:
            if sem is None:
                sem = eng
            self.cnt[sem] += inc
            tok = (sem, self.cnt[sem])
        else:
            tok = None
        if tok is not None:
            for pouts, pins in self.pending[eng]:
                for t in pins:
                    t.readers[tok[0]] = tok[1]
                for t in pouts:
                    t.writers[tok[0]] = tok[1]
            self.pending[eng] = []
            for t in ins:
                t.readers[tok[0]] = tok[1]
            for t in outs:
                t.writers = {tok[0]: tok[1]}
                t.readers = {}
        else:
            self.pending[eng].append((tuple(outs), tuple(ins)))
        self.records[eng].append((waits, fn, tok, inc, tag))
        return tok

    def raw_wait(self, eng, tok):
        s, v = tok
        if self.water[eng].get(s, 0) >= v:
            return
        self.water[eng][s] = v
        self.records[eng].append(([(s, v)], None, None, 0, None))

    def fence(self, eng, tensors):
        for t in tensors:
            for s, v in list(t.writers.items()) + list(t.readers.items()):
                self.raw_wait(eng, (s, v))

    def fence_all(self, tensors):
        for e in self.ENGINES:
            self.fence(e, tensors)

    def wait_all_dma(self, eng):
        for s in list(self.sem):
            if s.startswith("dch_") and self.cnt[s] > 0:
                self.raw_wait(eng, (s, self.cnt[s]))

    def dma(self, qeng, out, in_, ch=None, **kw):
        ot, oap = out if isinstance(out, tuple) else (out, out.ap)
        it, iap = in_ if isinstance(in_, tuple) else (in_, in_.ap)
        if ch is None:
            sb_t = ot if str(getattr(oap, "space", "")) != "DRAM" else it
            ch = sb_t.name
        sem_name = self.channel(ch)
        return self.emit(
            qeng,
            lambda e, oap=oap, iap=iap, kw=kw: e.dma_start(out=oap, in_=iap, **kw),
            outs=(ot,), ins=(it,), sem=sem_name, inc=16)

    def build(self):
        with self.nc.Block() as block:
            def mk(eng):
                def body(e):
                    for waits, fn, tok, inc, tag in self.records[eng]:
                        for s, v in waits:
                            e.wait_ge(self.sem[s], v)
                        if fn is None:
                            continue
                        inst = fn(e)
                        if tag:
                            inst.annotate(tag)
                        if tok is not None:
                            inst.then_inc(self.sem[tok[0]], inc)
                return body
            block.sync(mk("sync"))
            block.vector(mk("vector"))
            block.scalar(mk("scalar"))
            block.tensor(mk("tensor"))
            block.gpsimd(mk("gpsimd"))




class CFG:
    def __init__(self, S=8192, Dm=2048, I=5632, H=16, NC=8):
        self.S, self.Dm, self.I, self.H, self.NC = S, Dm, I, H, NC
        self.D = Dm // H
        assert self.D == 128
        self.hpc = H // NC
        assert self.hpc == 2
        self.BLK = S // NC
        assert self.BLK % 512 == 0
        self.FD = Dm // 128
        self.SG = S // 512
        assert S % 512 == 0 and I % 512 == 0
        self.OC = I // 512
        self.LC = I // 128
        self.TT8 = self.BLK // 128
        self.ISL = I // NC
        self.DSL = Dm // NC


TWO_PI = 2.0 * math.pi
CW_C1 = 6.28125
CW_C2 = float(np.float32(TWO_PI - CW_C1))
CW_C3 = float(TWO_PI - CW_C1 - CW_C2)
LN1E4 = math.log(10000.0)


def build_program(cfg, dbg=False):
    nc = bass.Bass("TRN2", target_bir_lowering=False)
    es = contextlib.ExitStack()
    with es:
        p = Prog(nc, es)
        S, Dm, I, NC = cfg.S, cfg.Dm, cfg.I, cfg.NC
        FD, SG, BLK, LC = cfg.FD, cfg.SG, cfg.BLK, cfg.LC
        pid = nc.partition_id()

        xT = p.dram("xT", [Dm, S], F32, kind="ExternalInput")
        x_blk = p.dram("x_blk", [BLK, Dm], F32, kind="ExternalInput")
        wqk = p.dram("wqk", [Dm, 512], F32, kind="ExternalInput")
        wv = p.dram("wv", [Dm, 256], F32, kind="ExternalInput")
        wout = p.dram("wout", [Dm, Dm], F32, kind="ExternalInput")
        wgT = p.dram("wgT", [Dm, I], F32, kind="ExternalInput")
        wuT = p.dram("wuT", [Dm, I], F32, kind="ExternalInput")
        wdT = p.dram("wdT", [I, Dm], F32, kind="ExternalInput")
        wg_sl = p.dram("wg_sl", [cfg.ISL, Dm], F32, kind="ExternalInput")
        wu_sl = p.dram("wu_sl", [cfg.ISL, Dm], F32, kind="ExternalInput")
        wd_sl = p.dram("wd_sl", [cfg.DSL, I], F32, kind="ExternalInput")
        gain2 = p.dram("gain2", [1, 2], F32, kind="ExternalInput")
        outT = p.dram("outT", [Dm, BLK], F32, kind="ExternalOutput")

        NSC = 2 * cfg.ISL + cfg.DSL
        cc_sc_in = p.dram("cc_sc_in", [1, NSC], F32)
        cc_sc_out = p.dram("cc_sc_out", [NC, NSC], F32)
        cc_y_in = p.dram("cc_y_in", [2 * 128, S], BF16)
        cc_y_out = p.dram("cc_y_out", [NC * 2 * 128, S], BF16)
        t_scr = p.dram("t_scr", [BLK, I], BF16)

        ident_b = p.sb("ident_b", [128, 128], BF16)
        ident_f = p.sb("ident_f", [128, 128], F32)
        ident_8 = p.sb("ident_8", [128, 128], FP8)
        ones_col_b = p.sb("ones_col_b", [128, 1], BF16)
        ones_row_b = p.sb("ones_row_b", [1, 128], BF16)
        ones_row_f = p.sb("ones_row_f", [1, 128], F32)
        one_f = p.sb("one_f", [1, 1], F32)
        eps_col = p.sb("eps_col", [128, 1], F32)
        epsn_row = p.sb("epsn_row", [1, 1], F32)
        gn = p.sb("gn", [1, 2], F32)

        setup = contextlib.ExitStack()
        with setup:
            ia = p.sb("ia", [128, 512], F32, scope=setup)
            ib = p.sb("ib", [128, 512], F32, scope=setup)
            ps_set = p.ps("ps_set", [128, 512], F32, scope=setup)
            p.emit("gpsimd", lambda g: g.iota(
                ia[:, 0:128], pattern=[[0, 128]], base=0, channel_multiplier=1,
                allow_small_or_imprecise_dtypes=True), outs=(ia,))
            p.emit("gpsimd", lambda g: g.iota(
                ib[:, 0:128], pattern=[[1, 128]], base=0, channel_multiplier=0,
                allow_small_or_imprecise_dtypes=True), outs=(ib,))
            for idt in (ident_b, ident_f, ident_8):
                p.emit("vector", lambda v, idt=idt: v.tensor_tensor(
                    idt.ap, ia[:, 0:128], ib[:, 0:128], ALU.is_equal),
                    outs=(idt,), ins=(ia, ib))
            p.emit("vector", lambda v: v.memset(ones_col_b.ap, 1.0), outs=(ones_col_b,))
            p.emit("vector", lambda v: v.memset(ones_row_b.ap, 1.0), outs=(ones_row_b,))
            p.emit("vector", lambda v: v.memset(ones_row_f.ap, 1.0), outs=(ones_row_f,))
            p.emit("vector", lambda v: v.memset(one_f.ap, 1.0), outs=(one_f,))
            p.emit("vector", lambda v: v.memset(eps_col.ap, 1e-5), outs=(eps_col,))
            p.emit("vector", lambda v: v.memset(epsn_row.ap, 1e-6), outs=(epsn_row,))
            p.dma("sync", gn, gain2, ch="gn")

            land = p.sb("sc_land", [128, max(Dm, I)], F32, scope=setup)
            cols = p.sb("sc_cols", [128, 16], F32, scope=setup)
            srow = p.sb("sc_srow", [1, NSC], F32, scope=setup)

            def scale_slice(src, nrows, ncols, off):
                nt = (nrows + 127) // 128
                for t in range(nt):
                    r0 = 128 * t
                    n = min(128, nrows - r0)
                    p.dma("sync", (land, land[0:n, 0:ncols]),
                          (src, src[r0:r0 + n, :]), ch="sc_land")
                    p.emit("vector", lambda v, n=n, t=t: v.reduce_max(
                        cols[0:n, t:t + 1], land[0:n, 0:ncols], axis=AX.X,
                        apply_absolute_value=True), outs=(cols,), ins=(land,))
                    p.emit("tensor", lambda tns, n=n, t=t: tns.matmul(
                        ps_set[0:1, 0:n], cols[0:n, t:t + 1], ident_f[0:n, 0:n],
                        start=True, stop=True), outs=(ps_set,), ins=(cols, ident_f))
                    p.emit("vector", lambda v, n=n, r0=r0, off=off: v.tensor_copy(
                        srow[0:1, off + r0:off + r0 + n], ps_set[0:1, 0:n]),
                        outs=(srow,), ins=(ps_set,))

            scale_slice(wg_sl, cfg.ISL, Dm, 0)
            scale_slice(wu_sl, cfg.ISL, Dm, cfg.ISL)
            scale_slice(wd_sl, cfg.DSL, I, 2 * cfg.ISL)
            p.dma("scalar", cc_sc_in, srow, ch="cc_stage")
            p.emit("gpsimd", lambda g: g.collective_compute(
                "AllGather", ALU.bypass, replica_groups=[list(range(NC))],
                ins=[cc_sc_in.ap.opt()], outs=[cc_sc_out.ap.opt()]),
                outs=(cc_sc_out,), ins=(cc_sc_in,), sem="cc")
            p.fence_all([ia, ib, land, cols, srow, ps_set])

        attn_es = contextlib.ExitStack()
        qhat = [p.sb("qhat0", [128, S], BF16, scope=attn_es),
                p.sb("qhat1", [128, S], BF16, scope=attn_es)]
        khat = [p.sb("khat0", [128, S], BF16, scope=attn_es),
                p.sb("khat1", [128, S], BF16, scope=attn_es)]
        v_tok = p.sb("v_tok", [128, (S // 128) * 256], BF16, scope=attn_es)

        p0b = contextlib.ExitStack()
        with p0b:
            Ct = p.sb("Ct", [128, S], BF16, scope=p0b)
            St = p.sb("St", [128, S], BF16, scope=p0b)
            rotm = p.sb("rotm", [128, 128], BF16, scope=p0b)
            arena_b = p.sb("arena_b", [1, 1024], BF16, scope=p0b)

            def cwqk_row(a, n):
                return arena_b[0:1, a:a + n]

            def cwv_row(a, n):
                return arena_b[0:1, 512 + a:512 + a + n]

            wqk_b = p.sb("wqk_b", [128, FD * 512], BF16, scope=p0b)
            wv_b = p.sb("wv_b", [128, FD * 256], BF16, scope=p0b)

            tr = contextlib.ExitStack()
            with tr:
                TSEG = min(S, 2048)
                th = p.sb("th", [64, TSEG], F32, scope=tr)
                tk = p.sb("tk", [64, TSEG], F32, scope=tr)
                tki = p.sb("tki", [64, TSEG], I32, scope=tr)
                tsin = p.sb("tsin", [64, TSEG], F32, scope=tr)
                tsin2 = p.sb("tsin2", [64, TSEG], F32, scope=tr)
                ivf = p.sb("ivf", [64, 1], F32, scope=tr)
                p.emit("gpsimd", lambda g: g.iota(
                    ivf.ap, pattern=[[0, 1]], base=0, channel_multiplier=1,
                    allow_small_or_imprecise_dtypes=True), outs=(ivf,))
                p.emit("scalar", lambda s_: s_.activation(
                    ivf.ap, ivf.ap, ACTF.Exp, scale=float(-LN1E4 / 64.0)),
                    outs=(ivf,), ins=(ivf,))
                for sg0 in range(0, S, TSEG):
                    ssl = slice(sg0, sg0 + TSEG)
                    p.emit("gpsimd", lambda g, sg0=sg0: g.iota(
                        th.ap, pattern=[[1, TSEG]], base=sg0,
                        channel_multiplier=0,
                        allow_small_or_imprecise_dtypes=True), outs=(th,))
                    p.emit("vector", lambda v: v.tensor_scalar(
                        th.ap, th.ap, ivf.ap, None, ALU.mult),
                        outs=(th,), ins=(th, ivf))
                    p.emit("vector", lambda v: v.tensor_scalar(
                        tk.ap, th.ap, float(1.0 / TWO_PI), None, ALU.mult),
                        outs=(tk,), ins=(th,))
                    p.emit("vector", lambda v: v.tensor_copy(tki.ap, tk.ap),
                           outs=(tki,), ins=(tk,))
                    p.emit("vector", lambda v: v.tensor_copy(tk.ap, tki.ap),
                           outs=(tk,), ins=(tki,))
                    for cc_ in (CW_C1, CW_C2, CW_C3):
                        p.emit("vector", lambda v, cc_=cc_: v.scalar_tensor_tensor(
                            th.ap, tk.ap, -cc_, th.ap, ALU.mult, ALU.add),
                            outs=(th,), ins=(th, tk))
                    p.emit("vector", lambda v: v.tensor_scalar(
                        tsin.ap, th.ap, float(math.pi), None, ALU.is_gt),
                        outs=(tsin,), ins=(th,))
                    p.emit("vector", lambda v: v.tensor_scalar(
                        tsin2.ap, th.ap, float(-math.pi), None, ALU.is_lt),
                        outs=(tsin2,), ins=(th,))
                    p.emit("vector", lambda v: v.tensor_tensor(
                        tsin2.ap, tsin2.ap, tsin.ap, ALU.subtract),
                        outs=(tsin2,), ins=(tsin2, tsin))
                    p.emit("vector", lambda v: v.scalar_tensor_tensor(
                        th.ap, tsin2.ap, float(TWO_PI), th.ap, ALU.mult,
                        ALU.add), outs=(th,), ins=(tsin2, th))
                    p.emit("scalar", lambda s_: s_.activation(
                        tsin.ap, th.ap, ACTF.Sin), outs=(tsin,), ins=(th,))
                    p.emit("vector", lambda v: v.tensor_scalar(
                        tk.ap, th.ap, float(math.pi / 2), None, ALU.add),
                        outs=(tk,), ins=(th,))
                    p.emit("vector", lambda v: v.tensor_scalar(
                        th.ap, tk.ap, float(-math.pi), None, ALU.is_lt),
                        outs=(th,), ins=(tk,))
                    p.emit("vector", lambda v: v.tensor_scalar(
                        tsin2.ap, tk.ap, float(math.pi), None, ALU.is_gt),
                        outs=(tsin2,), ins=(tk,))
                    p.emit("vector", lambda v: v.tensor_tensor(
                        th.ap, th.ap, tsin2.ap, ALU.subtract),
                        outs=(th,), ins=(th, tsin2))
                    p.emit("vector", lambda v: v.scalar_tensor_tensor(
                        tk.ap, th.ap, float(TWO_PI), tk.ap, ALU.mult, ALU.add),
                        outs=(tk,), ins=(th, tk))
                    p.emit("scalar", lambda s_: s_.activation(
                        tk.ap, tk.ap, ACTF.Sin), outs=(tk,), ins=(tk,))
                    p.dma("gpsimd", (Ct, Ct[0:64, ssl]), (tk, tk.ap), ch="trg0")
                    p.dma("gpsimd", (Ct, Ct[64:128, ssl]), (tk, tk.ap), ch="trg1")
                    p.dma("gpsimd", (St, St[64:128, ssl]), (tsin, tsin.ap),
                          ch="trg2")
                    p.emit("vector", lambda v: v.tensor_scalar(
                        tsin.ap, tsin.ap, -1.0, None, ALU.mult),
                        outs=(tsin,), ins=(tsin,))
                    p.dma("gpsimd", (St, St[0:64, ssl]), (tsin, tsin.ap),
                          ch="trg3")
                rmi = p.sb("rmi", [128, 256], F32, scope=tr)
                p.emit("gpsimd", lambda g: g.iota(
                    rmi[:, 0:128], pattern=[[0, 128]], base=0,
                    channel_multiplier=1,
                    allow_small_or_imprecise_dtypes=True), outs=(rmi,))
                p.emit("gpsimd", lambda g: g.iota(
                    rmi[:, 128:256], pattern=[[1, 128]], base=0,
                    channel_multiplier=0,
                    allow_small_or_imprecise_dtypes=True), outs=(rmi,))
                p.emit("vector", lambda v: v.tensor_tensor(
                    rmi[:, 0:128], rmi[:, 0:128], rmi[:, 128:256],
                    ALU.subtract), outs=(rmi,), ins=(rmi,))
                p.emit("scalar", lambda s_: s_.activation(
                    rmi[:, 0:128], rmi[:, 0:128], ACTF.Abs),
                    outs=(rmi,), ins=(rmi,))
                p.emit("vector", lambda v: v.tensor_scalar(
                    rotm.ap, rmi[:, 0:128], 64.0, None, ALU.is_equal),
                    outs=(rotm,), ins=(rmi,))
                p.fence_all([th, tk, tki, tsin, tsin2, ivf, rmi])

            p.dma("gpsimd", (wqk_b, wqk_b.ap.rearrange("p (f j) -> p f j", f=FD)),
                  (wqk, wqk.ap.rearrange("(f p) j -> p f j", p=128)), ch="wqk_b")
            p.dma("gpsimd", (wv_b, wv_b.ap.rearrange("p (f j) -> p f j", f=FD)),
                  (wv, wv.ap.rearrange("(f p) j -> p f j", p=128)), ch="wv_b")

            p1 = contextlib.ExitStack()
            with p1:
                xb = [p.sb(f"xb{i}", [128, FD * 512], BF16, scope=p1)
                      for i in range(2)]
                sqs = p.sb("sqs", [128, 512], BF16, scope=p1)
                qkraw = p.sb("qkraw", [128, 2048], F32, scope=p1)
                vtmp = p.sb("vtmp", [128, 512], F32, scope=p1)
                qn_t = p.sb("qn_t", [128, 512], BF16, scope=p1)
                negm_b = p.sb("negm_b", [1, 512], BF16, scope=p1)

                def negm_row(a, n):
                    return negm_b[0:1, 0:n]

                r512 = p.sb("r512", [1, 512], F32, scope=p1)
                r512b = p.sb("r512b", [1, 512], F32, scope=p1)
                repbuf = p.sb("repbuf", [128, 512], F32, scope=p1)
                icol = p.sb("icol", [128, 4], F32, scope=p1)
                ps_big = p.ps("ps_big", [128, 2048], F32, scope=p1)
                ps_v = p.ps("ps_v", [128, 1024], F32, scope=p1)
                ps_st = p.ps("ps_st", [1, 512], F32, scope=p1)
                ps_rep = p.ps("ps_rep", [128, 512], F32, scope=p1)

                for (wb, ncol, dst) in ((wqk_b, 512, cwqk_row),
                                        (wv_b, 256, cwv_row)):
                    for f in range(FD):
                        p.emit("tensor", lambda tns, wb=wb, f=f, ncol=ncol: tns.matmul(
                            ps_st[0:1, 0:ncol], ones_col_b.ap,
                            wb[:, ncol * f:ncol * (f + 1)],
                            start=(f == 0), stop=(f == FD - 1)),
                            outs=(ps_st,), ins=(ones_col_b, wb),
                            track=(f == FD - 1))
                    p.emit("vector", lambda v, ncol=ncol, dst=dst: v.tensor_copy(
                        dst(0, ncol), ps_st[0:1, 0:ncol]),
                        outs=(arena_b,), ins=(ps_st,))

                for t in range(SG):
                    sl = slice(512 * t, 512 * (t + 1))
                    b = t % 2
                    p.dma("gpsimd",
                          (xb[b], xb[b].ap.rearrange("p (f j) -> p f j", f=FD)),
                          (xT, xT[:, sl].rearrange("(f p) j -> p f j", p=128)),
                          ch=f"xb{b}")
                    for f in range(FD):
                        p.emit("tensor", lambda tns, b=b, f=f: tns.matmul(
                            ps_st.ap, ones_col_b.ap,
                            xb[b][:, 512 * f:512 * (f + 1)],
                            start=(f == 0), stop=(f == FD - 1)),
                            outs=(ps_st,), ins=(ones_col_b, xb[b]),
                            track=(f == FD - 1))
                    p.emit("scalar", lambda s_: s_.mul(
                        r512.ap, ps_st.ap, 1.0 / Dm), outs=(r512,), ins=(ps_st,))
                    for f in range(FD):
                        p.emit("scalar", lambda s_, b=b, f=f: s_.activation(
                            sqs.ap, xb[b][:, 512 * f:512 * (f + 1)],
                            ACTF.Square), outs=(sqs,), ins=(xb[b],))
                        p.emit("tensor", lambda tns, f=f: tns.matmul(
                            ps_st.ap, ones_col_b.ap, sqs.ap,
                            start=(f == 0), stop=(f == FD - 1)),
                            outs=(ps_st,), ins=(ones_col_b, sqs))
                    p.emit("scalar", lambda s_: s_.mul(
                        r512b.ap, ps_st.ap, 1.0 / Dm), outs=(r512b,), ins=(ps_st,))
                    p.emit("scalar", lambda s_: s_.activation(
                        repbuf[0:1, 0:512], r512.ap, ACTF.Square),
                        outs=(repbuf,), ins=(r512,))
                    p.emit("vector", lambda v: v.tensor_tensor(
                        r512b.ap, r512b.ap, repbuf[0:1, 0:512], ALU.subtract),
                        outs=(r512b,), ins=(r512b, repbuf))
                    p.emit("scalar", lambda s_: s_.activation(
                        r512b.ap, r512b.ap, ACTF.Sqrt, bias=eps_col[0:1, 0:1]),
                        outs=(r512b,), ins=(r512b, eps_col))
                    p.emit("vector", lambda v: v.reciprocal(
                        r512b.ap, r512b.ap), outs=(r512b,), ins=(r512b,))
                    p.emit("vector", lambda v: v.tensor_scalar(
                        negm_row(0, 512), r512.ap, -1.0, None, ALU.mult),
                        outs=(negm_b,), ins=(r512,))
                    for jc in range(4):
                        for f in range(FD):
                            p.emit("tensor", lambda tns, b=b, jc=jc, f=f: tns.matmul(
                                ps_big[:, 512 * jc:512 * (jc + 1)],
                                wqk_b[:, 512 * f + 128 * jc:512 * f + 128 * (jc + 1)],
                                xb[b][:, 512 * f:512 * (f + 1)],
                                start=(f == 0), stop=False),
                                outs=(ps_big,), ins=(wqk_b, xb[b]), track=False)
                        p.emit("tensor", lambda tns, jc=jc: tns.matmul(
                            ps_big[:, 512 * jc:512 * (jc + 1)],
                            cwqk_row(128 * jc, 128), negm_row(0, 512),
                            start=False, stop=True),
                            outs=(ps_big,), ins=(arena_b, negm_b), track=(jc == 3))
                    p.emit("vector", lambda v: v.tensor_copy(
                        qkraw.ap, ps_big.ap), outs=(qkraw,), ins=(ps_big,))
                    for vc in range(2):
                        for f in range(FD):
                            p.emit("tensor", lambda tns, b=b, vc=vc, f=f: tns.matmul(
                                ps_v[:, 512 * vc:512 * (vc + 1)],
                                wv_b[:, 256 * f + 128 * vc:256 * f + 128 * (vc + 1)],
                                xb[b][:, 512 * f:512 * (f + 1)],
                                start=(f == 0), stop=False),
                                outs=(ps_v,), ins=(wv_b, xb[b]), track=False)
                        p.emit("tensor", lambda tns, vc=vc: tns.matmul(
                            ps_v[:, 512 * vc:512 * (vc + 1)],
                            cwv_row(128 * vc, 128), negm_row(0, 512),
                            start=False, stop=True),
                            outs=(ps_v,), ins=(arena_b, negm_b), track=(vc == 1))

                    for jc in range(4):
                        p.emit("scalar", lambda s_, jc=jc: s_.activation(
                            sqs.ap, qkraw[:, 512 * jc:512 * (jc + 1)],
                            ACTF.Square), outs=(sqs,), ins=(qkraw,))
                        p.emit("tensor", lambda tns: tns.matmul(
                            ps_st.ap, ones_col_b.ap, sqs.ap,
                            start=True, stop=True),
                            outs=(ps_st,), ins=(ones_col_b, sqs))
                        p.emit("scalar", lambda s_: s_.activation(
                            r512.ap, ps_st.ap, ACTF.Sqrt,
                            bias=epsn_row[0:1, 0:1]),
                            outs=(r512,), ins=(ps_st, epsn_row))
                        p.emit("vector", lambda v: v.reciprocal(
                            r512.ap, r512.ap), outs=(r512,), ins=(r512,))
                        if jc < 2:
                            p.emit("vector", lambda v, jc=jc: v.tensor_scalar(
                                r512.ap, r512.ap, gn[0:1, jc:jc + 1], None,
                                ALU.mult), outs=(r512,), ins=(r512, gn))
                        p.emit("tensor", lambda tns: tns.matmul(
                            ps_rep.ap, ones_row_f.ap, r512.ap,
                            start=True, stop=True),
                            outs=(ps_rep,), ins=(ones_row_f, r512))
                        p.emit("vector", lambda v: v.tensor_copy(
                            repbuf.ap, ps_rep.ap), outs=(repbuf,), ins=(ps_rep,))
                        dst = (qhat[0], qhat[1], khat[0], khat[1])[jc]
                        roped = (jc % 2 == 0)
                        if not roped:
                            p.emit("vector", lambda v, jc=jc, dst=dst, sl=sl:
                                   v.tensor_tensor(
                                dst[:, sl], qkraw[:, 512 * jc:512 * (jc + 1)],
                                repbuf.ap, ALU.mult),
                                outs=(dst,), ins=(qkraw, repbuf))
                        else:
                            p.emit("vector", lambda v, jc=jc: v.tensor_tensor(
                                qn_t.ap, qkraw[:, 512 * jc:512 * (jc + 1)],
                                repbuf.ap, ALU.mult),
                                outs=(qn_t,), ins=(qkraw, repbuf))
                            p.emit("tensor", lambda tns: tns.matmul(
                                ps_rep.ap, rotm.ap, qn_t.ap,
                                start=True, stop=True),
                                outs=(ps_rep,), ins=(rotm, qn_t))
                            p.emit("vector", lambda v, dst=dst, sl=sl:
                                   v.tensor_tensor(
                                dst[:, sl], qn_t.ap, Ct[:, sl], ALU.mult),
                                outs=(dst,), ins=(qn_t, Ct))
                            p.emit("vector", lambda v, sl=sl: v.tensor_tensor(
                                sqs.ap, ps_rep.ap, St[:, sl], ALU.mult),
                                outs=(sqs,), ins=(ps_rep, St))
                            p.emit("vector", lambda v, dst=dst, sl=sl:
                                   v.tensor_tensor(
                                dst[:, sl], dst[:, sl], sqs.ap,
                                ALU.add), outs=(dst,), ins=(dst, sqs))
                    for q4 in range(4):
                        p.emit("tensor", lambda tns, q4=q4: tns.matmul(
                            ps_rep[:, 0:1], r512b[0:1, 128 * q4:128 * (q4 + 1)],
                            one_f.ap, start=True, stop=True),
                            outs=(ps_rep,), ins=(r512b, one_f))
                        p.emit("vector", lambda v, q4=q4: v.tensor_copy(
                            icol[:, q4:q4 + 1], ps_rep[:, 0:1]),
                            outs=(icol,), ins=(ps_rep,))
                    for vc in range(2):
                        p.emit("vector", lambda v, vc=vc: v.tensor_copy(
                            vtmp.ap, ps_v[:, 512 * vc:512 * (vc + 1)]),
                            outs=(vtmp,), ins=(ps_v,))
                        for q4 in range(4):
                            p.emit("tensor", lambda tns, vc=vc, q4=q4: tns.transpose(
                                ps_v[:, 512 * vc + 128 * q4:512 * vc + 128 * (q4 + 1)],
                                vtmp[:, 128 * q4:128 * (q4 + 1)],
                                ident_f.ap),
                                outs=(ps_v,), ins=(vtmp, ident_f),
                                track=(q4 == 3))
                        for q4 in range(4):
                            cs = 4 * t + q4
                            p.emit("vector", lambda v, vc=vc, q4=q4, cs=cs:
                                   v.tensor_scalar(
                                v_tok[:, 256 * cs + 128 * vc:256 * cs + 128 * (vc + 1)],
                                ps_v[:, 512 * vc + 128 * q4:512 * vc + 128 * (q4 + 1)],
                                icol[:, q4:q4 + 1], None, ALU.mult),
                                outs=(v_tok,), ins=(ps_v, icol))
                p.fence_all([xb[0], xb[1], sqs, qkraw, vtmp, qn_t, r512, r512b,
                             repbuf, icol, negm_b, ps_big, ps_v, ps_st, ps_rep])
            p.fence_all([Ct, St, rotm, wqk_b, wv_b])

        p2 = contextlib.ExitStack()
        with p2:
            masks = p.sb("masks", [128, 4 * 512], BF16, scope=p2)
            gbias = p.sb("gbias", [128, 2], F32, scope=p2)
            m_ia = p.sb("m_ia", [128, 1024], F32, scope=p2)
            E = [p.sb(f"E{i}", [128, 1024], BF16, scope=p2) for i in range(2)]
            s_sb = p.sb("s_sb", [1, 512], F32, scope=p2)
            rep2 = p.sb("rep2", [128, 512], F32, scope=p2)
            ystg = [p.sb(f"ystg{i}", [128, 512], BF16, scope=p2) for i in range(2)]
            ps_sl = [p.ps(f"ps_sl{i}", [128, 1024], F32, scope=p2)
                     for i in range(2)]
            ps_y = p.ps("ps_y", [128, 512], F32, scope=p2)
            ps_s = p.ps("ps_s", [1, 512], F32, scope=p2)
            ps_r2 = p.ps("ps_r2", [128, 512], F32, scope=p2)
            p.emit("gpsimd", lambda g: g.iota(
                m_ia[:, 0:512], pattern=[[1, 512]], base=0,
                channel_multiplier=0,
                allow_small_or_imprecise_dtypes=True), outs=(m_ia,))
            p.emit("gpsimd", lambda g: g.iota(
                m_ia[:, 512:1024], pattern=[[0, 512]], base=0,
                channel_multiplier=1,
                allow_small_or_imprecise_dtypes=True), outs=(m_ia,))
            p.emit("vector", lambda v: v.tensor_tensor(
                m_ia[:, 0:512], m_ia[:, 0:512], m_ia[:, 512:1024],
                ALU.subtract), outs=(m_ia,), ins=(m_ia,))
            for d in range(4):
                p.emit("vector", lambda v, d=d: v.tensor_scalar(
                    masks[:, 512 * d:512 * (d + 1)], m_ia[:, 0:512],
                    float(128 * d), None, ALU.is_ge), outs=(masks,), ins=(m_ia,))
            for h in range(2):
                p.emit("tensor", lambda tns, h=h: tns.matmul(
                    ps_r2[:, 0:1], ones_row_f.ap, gn[0:1, h:h + 1],
                    start=True, stop=True), outs=(ps_r2,), ins=(ones_row_f, gn))
                p.emit("scalar", lambda s_, h=h: s_.activation(
                    gbias[:, h:h + 1], ps_r2[:, 0:1], ACTF.Abs),
                    outs=(gbias,), ins=(ps_r2,))
                p.emit("vector", lambda v, h=h: v.tensor_scalar(
                    gbias[:, h:h + 1], gbias[:, h:h + 1], -1.0, None,
                    ALU.mult), outs=(gbias,), ins=(gbias,))
            for h in range(2):
                q, k = qhat[h], khat[h]
                for g in range(SG):
                    gsl = slice(512 * g, 512 * (g + 1))
                    npair = 2 * (g + 1)
                    for sp in range(npair):
                        sb_ = sp % 2
                        first = (sp == 0)
                        last = (sp == npair - 1)
                        for j in range(2):
                            tc = 2 * sp + j
                            p.emit("tensor", lambda tns, sb_=sb_, j=j, tc=tc,
                                   k=k, q=q, gsl=gsl: tns.matmul(
                                ps_sl[sb_][:, 512 * j:512 * (j + 1)],
                                k[:, 128 * tc:128 * (tc + 1)], q[:, gsl],
                                start=True, stop=True),
                                outs=(ps_sl[sb_],), ins=(k, q), track=(j == 1))
                        p.emit("scalar", lambda s_, sb_=sb_, h=h: s_.activation(
                            E[sb_].ap, ps_sl[sb_].ap, ACTF.Exp,
                            bias=gbias[:, h:h + 1]),
                            outs=(E[sb_],), ins=(ps_sl[sb_], gbias))
                        for j in range(2):
                            d = 2 * sp + j - 4 * g
                            if d >= 0:
                                p.emit("vector", lambda v, sb_=sb_, j=j, d=d:
                                       v.tensor_tensor(
                                    E[sb_][:, 512 * j:512 * (j + 1)],
                                    E[sb_][:, 512 * j:512 * (j + 1)],
                                    masks[:, 512 * d:512 * (d + 1)], ALU.mult),
                                    outs=(E[sb_],), ins=(E[sb_], masks))
                        for j in range(2):
                            tc = 2 * sp + j
                            p.emit("tensor", lambda tns, sb_=sb_, j=j, tc=tc,
                                   h=h, first=first, last=last: tns.matmul(
                                ps_y.ap,
                                v_tok[:, 256 * tc + 128 * h:256 * tc + 128 * (h + 1)],
                                E[sb_][:, 512 * j:512 * (j + 1)],
                                start=(first and j == 0), stop=(last and j == 1)),
                                outs=(ps_y,), ins=(v_tok, E[sb_]),
                                track=(last and j == 1))
                            p.emit("tensor", lambda tns, sb_=sb_, j=j,
                                   first=first, last=last: tns.matmul(
                                ps_s.ap, ones_col_b.ap,
                                E[sb_][:, 512 * j:512 * (j + 1)],
                                start=(first and j == 0), stop=(last and j == 1)),
                                outs=(ps_s,), ins=(ones_col_b, E[sb_]),
                                track=(last and j == 1))
                    p.emit("vector", lambda v: v.reciprocal(
                        s_sb.ap, ps_s.ap), outs=(s_sb,), ins=(ps_s,))
                    p.emit("tensor", lambda tns: tns.matmul(
                        ps_r2.ap, ones_row_f.ap, s_sb.ap, start=True, stop=True),
                        outs=(ps_r2,), ins=(ones_row_f, s_sb))
                    p.emit("vector", lambda v: v.tensor_copy(
                        rep2.ap, ps_r2.ap), outs=(rep2,), ins=(ps_r2,))
                    gy = g % 2
                    p.emit("vector", lambda v, gy=gy: v.tensor_tensor(
                        ystg[gy].ap, ps_y.ap, rep2.ap, ALU.mult),
                        outs=(ystg[gy],), ins=(ps_y, rep2))
                    p.dma("scalar", (cc_y_in,
                                     cc_y_in[128 * h:128 * (h + 1), gsl]),
                          (ystg[gy], ystg[gy].ap), ch=f"ystg{gy}")
            p.fence_all([masks, gbias, m_ia, E[0], E[1], s_sb, rep2,
                         ystg[0], ystg[1], ps_sl[0], ps_sl[1], ps_y, ps_s,
                         ps_r2])

        p.emit("gpsimd", lambda g: g.collective_compute(
            "AllGather", ALU.bypass, replica_groups=[list(range(NC))],
            ins=[cc_y_in.ap.opt()], outs=[cc_y_out.ap.opt()]),
            outs=(cc_y_out,), ins=(cc_y_in,), sem="cc")

        if dbg:
            dbg_q = p.dram("dbg_q", [128, S], BF16, kind="ExternalOutput")
            dbg_k = p.dram("dbg_k", [128, S], BF16, kind="ExternalOutput")
            dbg_v = p.dram("dbg_v", [128, 2 * (S // 128) * 256 // 2], BF16,
                           kind="ExternalOutput")
            dbg_y = p.dram("dbg_y", [NC * 256, S], BF16, kind="ExternalOutput")
            p.dma("scalar", dbg_q, qhat[0], ch="dbgq")
            p.dma("scalar", dbg_k, khat[0], ch="dbgk")
            p.dma("scalar", (dbg_v, dbg_v.ap), (v_tok, v_tok.ap), ch="dbgv")
            p.dma("scalar", dbg_y, cc_y_out, ch="dbgy")
        p.fence_all([qhat[0], qhat[1], khat[0], khat[1], v_tok])
        attn_es.close()

        p3 = contextlib.ExitStack()
        with p3:
            arena_d = p.sb("arena_d", [1, Dm], F32, scope=p3)

            def sd_row(a, n):
                return arena_d[0:1, a:a + n]

            qtT = p.sb("qtT", [128, LC * BLK], FP8, scope=p3)
            rtcol = p.sb("rtcol", [128, cfg.TT8], F32, scope=p3)
            acol = p.sb("acol", [128, cfg.TT8], F32, scope=p3)
            rep_rt = p.sb("rep_rt", [128, BLK], F32, scope=p3)
            blkrow = p.sb("blkrow", [1, 2 * BLK], F32, scope=p3)
            dcol16 = p.sb("dcol16", [128, FD + 4], F32, scope=p3)

            mid = contextlib.ExitStack()
            arena_su = p.sb("arena_su", [1, 2 * I], F32, scope=mid)

            def sg_row(a, n):
                return arena_su[0:1, a:a + n]

            def su_row(a, n):
                return arena_su[0:1, I + a:I + a + n]

            p.dma("sync", (arena_su, sg_row(0, I)),
                  (cc_sc_out, cc_sc_out[:, 0:cfg.ISL]), ch="sc_back")
            p.dma("sync", (arena_su, su_row(0, I)),
                  (cc_sc_out, cc_sc_out[:, cfg.ISL:2 * cfg.ISL]), ch="sc_back2")
            p.dma("sync", (arena_d, sd_row(0, Dm)),
                  (cc_sc_out, cc_sc_out[:, 2 * cfg.ISL:NSC]), ch="sc_back3")
            for (rw, n, ar) in ((sg_row, I, arena_su), (su_row, I, arena_su),
                                (sd_row, Dm, arena_d)):
                p.emit("vector", lambda v, rw=rw, n=n: v.reciprocal(
                    rw(0, n), rw(0, n)), outs=(ar,), ins=(ar,))
                p.emit("vector", lambda v, rw=rw, n=n: v.tensor_scalar(
                    rw(0, n), rw(0, n), 224.0, None, ALU.mult),
                    outs=(ar,), ins=(ar,))

            qxT = p.sb("qxT", [128, FD * BLK], FP8, scope=mid)

            pre = contextlib.ExitStack()
            with pre:
                xtok = p.sb("xtok", [128, Dm], F32, scope=pre)
                ttmp = p.sb("ttmp", [128, Dm], F32, scope=pre)
                tsq = p.sb("tsq", [128, Dm], BF16, scope=pre)
                rep_mq = p.sb("rep_mq", [128, 2 * BLK], F32, scope=pre)
                landx = p.sb("landx", [128, (FD // 2) * BLK], F32, scope=pre)
                psA = p.ps("psA", [128, 512], F32, scope=pre)
                for tt in range(cfg.TT8):
                    p.dma("sync", (xtok, xtok.ap),
                          (x_blk, x_blk[128 * tt:128 * (tt + 1), :]), ch="xtok")
                    p.emit("vector", lambda v, tt=tt: v.reduce_sum(
                        acol[:, tt:tt + 1], xtok.ap, axis=AX.X),
                        outs=(acol,), ins=(xtok,))
                    p.emit("vector", lambda v, tt=tt: v.tensor_scalar(
                        acol[:, tt:tt + 1], acol[:, tt:tt + 1], 1.0 / Dm, None,
                        ALU.mult), outs=(acol,), ins=(acol,))
                    p.emit("vector", lambda v, tt=tt: v.tensor_scalar(
                        ttmp.ap, xtok.ap, acol[:, tt:tt + 1], None,
                        ALU.subtract), outs=(ttmp,), ins=(xtok, acol))
                    p.emit("vector", lambda v, tt=tt: v.reduce_max(
                        rtcol[:, tt:tt + 1], ttmp.ap, axis=AX.X,
                        apply_absolute_value=True), outs=(rtcol,), ins=(ttmp,))
                    p.emit("scalar", lambda s_: s_.activation(
                        tsq.ap, xtok.ap, ACTF.Square), outs=(tsq,), ins=(xtok,))
                    p.emit("vector", lambda v: v.reduce_sum(
                        dcol16[:, FD:FD + 1], tsq.ap, axis=AX.X),
                        outs=(dcol16,), ins=(tsq,))
                    p.emit("vector", lambda v: v.tensor_scalar(
                        dcol16[:, FD:FD + 1], dcol16[:, FD:FD + 1], 1.0 / Dm,
                        None, ALU.mult), outs=(dcol16,), ins=(dcol16,))
                    p.emit("scalar", lambda s_, tt=tt: s_.activation(
                        dcol16[:, FD + 1:FD + 2], acol[:, tt:tt + 1],
                        ACTF.Square), outs=(dcol16,), ins=(acol,))
                    p.emit("vector", lambda v: v.tensor_tensor(
                        dcol16[:, FD:FD + 1], dcol16[:, FD:FD + 1],
                        dcol16[:, FD + 1:FD + 2], ALU.subtract),
                        outs=(dcol16,), ins=(dcol16,))
                    p.emit("scalar", lambda s_: s_.activation(
                        dcol16[:, FD:FD + 1], dcol16[:, FD:FD + 1], ACTF.Sqrt,
                        bias=eps_col[:, 0:1]), outs=(dcol16,), ins=(dcol16, eps_col))
                    p.emit("vector", lambda v: v.reciprocal(
                        dcol16[:, FD:FD + 1], dcol16[:, FD:FD + 1]),
                        outs=(dcol16,), ins=(dcol16,))
                    p.emit("tensor", lambda tns, tt=tt: tns.matmul(
                        psA[0:1, 0:128], acol[:, tt:tt + 1], ident_f.ap,
                        start=True, stop=True), outs=(psA,), ins=(acol, ident_f))
                    p.emit("vector", lambda v, tt=tt: v.tensor_copy(
                        blkrow[0:1, 128 * tt:128 * (tt + 1)], psA[0:1, 0:128]),
                        outs=(blkrow,), ins=(psA,))
                    p.emit("vector", lambda v, tt=tt: v.reciprocal(
                        dcol16[:, FD + 2:FD + 3], rtcol[:, tt:tt + 1]),
                        outs=(dcol16,), ins=(rtcol,))
                    p.emit("vector", lambda v: v.tensor_scalar(
                        dcol16[:, FD + 2:FD + 3], dcol16[:, FD + 2:FD + 3],
                        224.0, None, ALU.mult), outs=(dcol16,), ins=(dcol16,))
                    p.emit("tensor", lambda tns: tns.matmul(
                        psA[0:1, 0:128], dcol16[:, FD + 2:FD + 3], ident_f.ap,
                        start=True, stop=True), outs=(psA,), ins=(dcol16, ident_f))
                    p.emit("vector", lambda v, tt=tt: v.tensor_copy(
                        blkrow[0:1, BLK + 128 * tt:BLK + 128 * (tt + 1)],
                        psA[0:1, 0:128]), outs=(blkrow,), ins=(psA,))
                    p.emit("vector", lambda v, tt=tt: v.tensor_scalar(
                        acol[:, tt:tt + 1], rtcol[:, tt:tt + 1],
                        dcol16[:, FD:FD + 1], 1.0 / 224.0, ALU.mult, ALU.mult),
                        outs=(acol,), ins=(rtcol, dcol16))
                for hf in range(2 * BLK // 512):
                    p.emit("tensor", lambda tns, hf=hf: tns.matmul(
                        psA.ap, ones_row_f.ap,
                        blkrow[0:1, 512 * hf:512 * (hf + 1)],
                        start=True, stop=True), outs=(psA,), ins=(ones_row_f, blkrow))
                    p.emit("vector", lambda v, hf=hf: v.tensor_copy(
                        rep_mq[:, 512 * hf:512 * (hf + 1)], psA.ap),
                        outs=(rep_mq,), ins=(psA,))
                FH = FD // 2
                for hx in range(2):
                    p.dma("sync",
                          (landx, landx.ap.rearrange("p (f u) -> p f u", f=FH)),
                          (xT, xT[128 * FH * hx:128 * FH * (hx + 1),
                                  bass.ds(pid * BLK, BLK)].rearrange(
                              "(f p) u -> p f u", p=128)), ch="landx")
                    for fl in range(FH):
                        f = FH * hx + fl
                        p.emit("vector", lambda v, fl=fl: v.tensor_tensor(
                            landx[:, BLK * fl:BLK * (fl + 1)],
                            landx[:, BLK * fl:BLK * (fl + 1)], rep_mq[:, 0:BLK],
                            ALU.subtract), outs=(landx,), ins=(landx, rep_mq))
                        p.emit("vector", lambda v, f=f, fl=fl: v.tensor_tensor(
                            qxT[:, BLK * f:BLK * (f + 1)],
                            landx[:, BLK * fl:BLK * (fl + 1)],
                            rep_mq[:, BLK:2 * BLK], ALU.mult),
                            outs=(qxT,), ins=(landx, rep_mq))
                p.fence_all([xtok, ttmp, tsq, rep_mq, landx, psA])

            gu = contextlib.ExitStack()
            with gu:
                gland = [p.sb(f"gland{i}", [128, 1024], F32, scope=gu)
                         for i in range(2)]
                qwg = p.sb("qwg", [128, FD * 1024], FP8, scope=gu)
                qwu = p.sb("qwu", [128, FD * 1024], FP8, scope=gu)
                repw = p.sb("repw", [128, 2 * 1024], F32, scope=gu)
                sgu = p.sb("sgu", [128, 512], F32, scope=gu)
                uu = p.sb("uu", [128, 512], F32, scope=gu)
                tstg = [p.sb(f"tstg{i}", [128, 1024], BF16, scope=gu)
                        for i in range(2)]
                ps_gu = [p.ps(f"ps_gu{i}", [128, 2048], F32, scope=gu)
                         for i in range(2)]
                first_og = True
                oc = 0
                while oc < cfg.OC:
                    nog = min(2, cfg.OC - oc)
                    ow = 512 * nog
                    for (wsrc, qdst, srw, ri, qeng) in (
                            (wgT, qwg, sg_row, 0, "vector"),
                            (wuT, qwu, su_row, 1, "vector")):
                        for j in range(nog):
                            p.emit("tensor", lambda tns, oc=oc, j=j, srw=srw: tns.matmul(
                                ps_gu[1][:, 0:512], ones_row_f.ap,
                                srw(512 * (oc + j), 512), start=True, stop=True),
                                outs=(ps_gu[1],), ins=(ones_row_f, arena_su))
                            p.emit("vector", lambda v, ri=ri, j=j: v.tensor_copy(
                                repw[:, 1024 * ri + 512 * j:1024 * ri + 512 * (j + 1)],
                                ps_gu[1][:, 0:512]), outs=(repw,), ins=(ps_gu[1],))
                        for f in range(FD):
                            b = f % 2
                            p.dma("sync", (gland[b], gland[b][:, 0:ow]),
                                  (wsrc, wsrc[128 * f:128 * (f + 1),
                                              512 * oc:512 * oc + ow]),
                                  ch=f"gland{b}")
                            p.emit(qeng, lambda v, b=b, f=f, ri=ri, ow=ow,
                                   qdst=qdst: v.tensor_tensor(
                                qdst[:, 1024 * f:1024 * f + ow],
                                gland[b][:, 0:ow],
                                repw[:, 1024 * ri:1024 * ri + ow], ALU.mult),
                                outs=(qdst,), ins=(gland[b], repw))
                    for ri in range(2):
                        p.emit("vector", lambda v, ri=ri, ow=ow: v.reciprocal(
                            repw[:, 1024 * ri:1024 * ri + ow],
                            repw[:, 1024 * ri:1024 * ri + ow]),
                            outs=(repw,), ins=(repw,))
                    for tt in range(cfg.TT8):
                        sbk = tt % 2
                        for f in range(FD):
                            for j in range(nog):
                                for wi, qdst in ((0, qwg), (1, qwu)):
                                    p.emit("tensor", lambda tns, sbk=sbk, f=f,
                                           j=j, wi=wi, qdst=qdst, tt=tt: tns.matmul(
                                        ps_gu[sbk][:, 1024 * j + 512 * wi:
                                                   1024 * j + 512 * (wi + 1)],
                                        qxT[:, BLK * f + 128 * tt:
                                            BLK * f + 128 * (tt + 1)],
                                        qdst[:, 1024 * f + 512 * j:
                                             1024 * f + 512 * (j + 1)],
                                        start=(f == 0), stop=(f == FD - 1)),
                                        outs=(ps_gu[sbk],), ins=(qxT, qdst),
                                        track=(f == FD - 1 and j == nog - 1
                                               and wi == 1))
                        ts_ = tt % 2
                        for j in range(nog):
                            p.emit("vector", lambda v, sbk=sbk, j=j, tt=tt:
                                   v.scalar_tensor_tensor(
                                uu.ap, ps_gu[sbk][:, 1024 * j + 512:1024 * (j + 1)],
                                acol[:, tt:tt + 1],
                                repw[:, 1024 + 512 * j:1024 + 512 * (j + 1)],
                                ALU.mult, ALU.mult),
                                outs=(uu,), ins=(ps_gu[sbk], acol, repw))
                            p.emit("vector", lambda v, sbk=sbk, j=j: v.tensor_tensor(
                                sgu.ap, ps_gu[sbk][:, 1024 * j:1024 * j + 512],
                                repw[:, 512 * j:512 * (j + 1)], ALU.mult),
                                outs=(sgu,), ins=(ps_gu[sbk], repw))
                            p.emit("scalar", lambda s_, tt=tt: s_.activation(
                                sgu.ap, sgu.ap, ACTF.Silu,
                                scale=acol[:, tt:tt + 1]),
                                outs=(sgu,), ins=(sgu, acol))
                            p.emit("vector", lambda v, ts_=ts_, j=j: v.tensor_tensor(
                                tstg[ts_][:, 512 * j:512 * (j + 1)], sgu.ap,
                                uu.ap, ALU.mult), outs=(tstg[ts_],), ins=(sgu, uu))
                        p.emit("vector", lambda v, ts_=ts_, tt=tt, ow=ow,
                               first_og=first_og: v.reduce_max(
                            dcol16[:, FD + 3:FD + 4], tstg[ts_][:, 0:ow],
                            axis=AX.X, apply_absolute_value=True),
                            outs=(dcol16,), ins=(tstg[ts_],))
                        if first_og:
                            p.emit("vector", lambda v, tt=tt: v.tensor_copy(
                                rtcol[:, tt:tt + 1], dcol16[:, FD + 3:FD + 4]),
                                outs=(rtcol,), ins=(dcol16,))
                        else:
                            p.emit("vector", lambda v, tt=tt: v.tensor_tensor(
                                rtcol[:, tt:tt + 1], rtcol[:, tt:tt + 1],
                                dcol16[:, FD + 3:FD + 4], ALU.max),
                                outs=(rtcol,), ins=(rtcol, dcol16))
                        p.dma("scalar", (t_scr,
                                         t_scr[128 * tt:128 * (tt + 1),
                                               512 * oc:512 * oc + ow]),
                              (tstg[ts_], tstg[ts_][:, 0:ow]), ch=f"tstg{ts_}")
                    first_og = False
                    oc += nog
                p.fence_all([gland[0], gland[1], qwg, qwu, repw, sgu, uu,
                             tstg[0], tstg[1], ps_gu[0], ps_gu[1]])
            p.fence_all([arena_su, qxT])
            mid.close()

            qt_s = contextlib.ExitStack()
            with qt_s:
                tbuf = [p.sb(f"tbuf{i}", [128, I], BF16, scope=qt_s)
                        for i in range(2)]
                rep_qs = p.sb("rep_qs", [128, BLK], F32, scope=qt_s)
                ps_t8 = [p.ps(f"ps_t8{i}", [128, 512], BF16, scope=qt_s)
                         for i in range(2)]
                psB = p.ps("psB", [128, 512], F32, scope=qt_s)
                for (dst, inv) in ((rep_rt, False), (rep_qs, True)):
                    for tt in range(cfg.TT8):
                        if inv:
                            p.emit("vector", lambda v, tt=tt: v.reciprocal(
                                dcol16[:, FD + 3:FD + 4], rtcol[:, tt:tt + 1]),
                                outs=(dcol16,), ins=(rtcol,))
                            p.emit("vector", lambda v: v.tensor_scalar(
                                dcol16[:, FD + 3:FD + 4],
                                dcol16[:, FD + 3:FD + 4], 224.0, None,
                                ALU.mult), outs=(dcol16,), ins=(dcol16,))
                        else:
                            p.emit("vector", lambda v, tt=tt: v.tensor_scalar(
                                dcol16[:, FD + 3:FD + 4], rtcol[:, tt:tt + 1],
                                1.0 / 224.0, None, ALU.mult),
                                outs=(dcol16,), ins=(rtcol,))
                        p.emit("tensor", lambda tns: tns.matmul(
                            psB[0:1, 0:128], dcol16[:, FD + 3:FD + 4],
                            ident_f.ap, start=True, stop=True),
                            outs=(psB,), ins=(dcol16, ident_f))
                        p.emit("vector", lambda v, tt=tt: v.tensor_copy(
                            blkrow[0:1, 128 * tt:128 * (tt + 1)],
                            psB[0:1, 0:128]), outs=(blkrow,), ins=(psB,))
                    for hf in range(BLK // 512):
                        p.emit("tensor", lambda tns, hf=hf: tns.matmul(
                            psB.ap, ones_row_f.ap,
                            blkrow[0:1, 512 * hf:512 * (hf + 1)],
                            start=True, stop=True),
                            outs=(psB,), ins=(ones_row_f, blkrow))
                        p.emit("vector", lambda v, hf=hf, dst=dst: v.tensor_copy(
                            dst[:, 512 * hf:512 * (hf + 1)], psB.ap),
                            outs=(dst,), ins=(psB,))
                for tt in range(cfg.TT8):
                    b = tt % 2
                    p.dma("sync", (tbuf[b], tbuf[b].ap),
                          (t_scr, t_scr[128 * tt:128 * (tt + 1), :]),
                          ch=f"tbuf{b}")
                    for lc in range(LC):
                        k4 = lc % 4
                        b8 = (lc // 4) % 2
                        p.emit("tensor", lambda tns, lc=lc, k4=k4, b8=b8,
                               b=b: tns.transpose(
                            ps_t8[b8][:, 128 * k4:128 * (k4 + 1)],
                            tbuf[b][:, 128 * lc:128 * (lc + 1)], ident_b.ap),
                            outs=(ps_t8[b8],), ins=(tbuf[b], ident_b),
                            track=(k4 == 3))
                        if k4 == 3:
                            base = lc - 3
                            for a4 in range(4):
                                p.emit("vector", lambda v, b8=b8, base=base,
                                       tt=tt, a4=a4: v.tensor_tensor(
                                    qtT[:, BLK * (base + a4) + 128 * tt:
                                        BLK * (base + a4) + 128 * (tt + 1)],
                                    ps_t8[b8][:, 128 * a4:128 * (a4 + 1)],
                                    rep_qs[:, 128 * tt:128 * (tt + 1)],
                                    ALU.mult), outs=(qtT,), ins=(ps_t8[b8], rep_qs))
                p.fence_all([tbuf[0], tbuf[1], rep_qs, ps_t8[0], ps_t8[1], psB])

            p3c = contextlib.ExitStack()
            with p3c:
                Yblk = p.sb("Yblk", [128, FD * BLK], BF16, scope=p3c)
                land3 = [p.sb(f"land3_{i}", [128, 512], F32, scope=p3c)
                         for i in range(2)]
                qwd = p.sb("qwd", [128, LC * 512], FP8, scope=p3c)
                woutb = p.sb("woutb", [128, FD * 512], BF16, scope=p3c)
                repd = p.sb("repd", [128, 512], F32, scope=p3c)
                xres = [p.sb(f"xres{i}", [128, 4 * 512], F32, scope=p3c)
                        for i in range(2)]
                ev = [p.sb(f"ev{i}", [128, 512], F32, scope=p3c)
                      for i in range(2)]
                ps_at = [p.ps(f"ps_at{i}", [128, 1024], F32, scope=p3c)
                         for i in range(2)]
                ps_ff = [p.ps(f"ps_ff{i}", [128, 1024], F32, scope=p3c)
                         for i in range(2)]
                p.dma("sync",
                      (Yblk, Yblk.ap.rearrange("p (f u) -> p f u", f=FD)),
                      (cc_y_out, cc_y_out[:, bass.ds(pid * BLK, BLK)].rearrange(
                          "(f p) u -> p f u", p=128)), ch="Yblk")
                for ot in range(FD):
                    p.emit("tensor", lambda tns, ot=ot: tns.matmul(
                        ps_at[0][:, 0:1], sd_row(128 * ot, 128), one_f.ap,
                        start=True, stop=True), outs=(ps_at[0],), ins=(arena_d, one_f))
                    p.emit("vector", lambda v, ot=ot: v.reciprocal(
                        dcol16[:, ot:ot + 1], ps_at[0][:, 0:1]),
                        outs=(dcol16,), ins=(ps_at[0],))
                NQ = FD // 4
                for oq in range(NQ):
                    osl = slice(512 * oq, 512 * (oq + 1))
                    p.emit("tensor", lambda tns, oq=oq: tns.matmul(
                        ps_at[0][:, 0:512], ones_row_f.ap, sd_row(512 * oq, 512),
                        start=True, stop=True), outs=(ps_at[0],), ins=(ones_row_f, arena_d))
                    p.emit("vector", lambda v: v.tensor_copy(
                        repd.ap, ps_at[0][:, 0:512]), outs=(repd,), ins=(ps_at[0],))
                    for lc in range(LC):
                        b = lc % 2
                        p.dma("sync", (land3[b], land3[b].ap),
                              (wdT, wdT[128 * lc:128 * (lc + 1), osl]),
                              ch=f"land3_{b}")
                        p.emit("vector", lambda v, b=b, lc=lc: v.tensor_tensor(
                            qwd[:, 512 * lc:512 * (lc + 1)], land3[b].ap,
                            repd.ap, ALU.mult), outs=(qwd,), ins=(land3[b], repd))
                    p.dma("gpsimd",
                          (woutb, woutb.ap.rearrange("p (f j) -> p f j", f=FD)),
                          (wout, wout[:, osl].rearrange("(f p) j -> p f j",
                                                        p=128)), ch="woutb")
                    for tt2x in range(BLK // 512):
                        p.dma("sync", (xres[tt2x % 2],
                                       xres[tt2x % 2].ap.rearrange(
                                           "p (a u) -> p a u", a=4)),
                              (xT, xT[512 * oq:512 * (oq + 1),
                                      bass.ds(pid * BLK + 512 * tt2x, 512)]
                               .rearrange("(a p) u -> p a u", p=128)),
                              ch=f"xres{tt2x % 2}")
                    for j in range(4):
                        ot = 4 * oq + j
                        par = j % 2
                        for tt2 in range(BLK // 512):
                            for f in range(FD):
                                p.emit("tensor", lambda tns, par=par, tt2=tt2,
                                       f=f, j=j: tns.matmul(
                                    ps_at[par][:, 512 * tt2:512 * (tt2 + 1)],
                                    woutb[:, 512 * f + 128 * j:
                                          512 * f + 128 * (j + 1)],
                                    Yblk[:, BLK * f + 512 * tt2:
                                         BLK * f + 512 * (tt2 + 1)],
                                    start=(f == 0), stop=(f == FD - 1)),
                                    outs=(ps_at[par],), ins=(woutb, Yblk),
                                    track=(f == FD - 1))
                            for lc in range(LC):
                                p.emit("tensor", lambda tns, par=par, tt2=tt2,
                                       lc=lc, j=j: tns.matmul(
                                    ps_ff[par][:, 512 * tt2:512 * (tt2 + 1)],
                                    qwd[:, 512 * lc + 128 * j:
                                        512 * lc + 128 * (j + 1)],
                                    qtT[:, BLK * lc + 512 * tt2:
                                        BLK * lc + 512 * (tt2 + 1)],
                                    start=(lc == 0), stop=(lc == LC - 1)),
                                    outs=(ps_ff[par],), ins=(qwd, qtT),
                                    track=(lc == LC - 1))
                            eslot = (2 * j + tt2) % 2
                            p.emit("vector", lambda v, par=par, tt2=tt2, ot=ot,
                                   eslot=eslot: v.scalar_tensor_tensor(
                                ev[eslot].ap,
                                ps_ff[par][:, 512 * tt2:512 * (tt2 + 1)],
                                dcol16[:, ot:ot + 1],
                                rep_rt[:, 512 * tt2:512 * (tt2 + 1)],
                                ALU.mult, ALU.mult),
                                outs=(ev[eslot],), ins=(ps_ff[par], dcol16, rep_rt))
                            p.emit("vector", lambda v, par=par, tt2=tt2,
                                   eslot=eslot: v.tensor_tensor(
                                ev[eslot].ap, ev[eslot].ap,
                                ps_at[par][:, 512 * tt2:512 * (tt2 + 1)],
                                ALU.add), outs=(ev[eslot],), ins=(ev[eslot], ps_at[par]))
                            p.emit("vector", lambda v, eslot=eslot, j=j,
                                   tt2=tt2: v.tensor_tensor(
                                ev[eslot].ap, ev[eslot].ap,
                                xres[tt2 % 2][:, 512 * j:512 * (j + 1)],
                                ALU.add), outs=(ev[eslot],),
                                ins=(ev[eslot], xres[tt2 % 2]))
                            p.dma("scalar", (outT,
                                             outT[128 * ot:128 * (ot + 1),
                                                  512 * tt2:512 * (tt2 + 1)]),
                                  (ev[eslot], ev[eslot].ap), ch=f"ev{eslot}")
                p.fence_all([Yblk, land3[0], land3[1], qwd, woutb, repd,
                             xres[0], xres[1], ev[0], ev[1], ps_at[0],
                             ps_at[1], ps_ff[0], ps_ff[1]])

        p.wait_all_dma("sync")
        p.build()
    return nc




def make_in_maps(cfg, x, w_qkv, w_out, q_gain, w_gate, w_up, w_down,
                 up_idx, down_idx):
    S, Dm, I, NC, H = cfg.S, cfg.Dm, cfg.I, cfg.NC, cfg.H
    x2 = np.ascontiguousarray(np.asarray(x, np.float32).reshape(S, Dm))
    xT = np.ascontiguousarray(x2.T)
    wq3 = np.asarray(w_qkv, np.float32).reshape(Dm, 3, H, 128)
    w_out = np.ascontiguousarray(np.asarray(w_out, np.float32))
    sinv = np.argsort(np.asarray(down_idx).astype(np.int64))
    wgT = np.ascontiguousarray(np.asarray(w_gate, np.float32).T)
    wuT = np.ascontiguousarray(np.asarray(w_up, np.float32).T)
    wdpT = np.ascontiguousarray(np.asarray(w_down, np.float32)[:, sinv].T)
    w_gate = np.ascontiguousarray(np.asarray(w_gate, np.float32))
    w_up = np.ascontiguousarray(np.asarray(w_up, np.float32))
    w_down = np.ascontiguousarray(np.asarray(w_down, np.float32))
    q_gain = np.asarray(q_gain, np.float32)
    in_maps = []
    for c in range(NC):
        h0, h1 = 2 * c, 2 * c + 1
        wqk_c = np.ascontiguousarray(np.concatenate(
            [wq3[:, 0, h0], wq3[:, 0, h1], wq3[:, 1, h0], wq3[:, 1, h1]],
            axis=1))
        wv_c = np.ascontiguousarray(np.concatenate(
            [wq3[:, 2, h0], wq3[:, 2, h1]], axis=1))
        in_maps.append({
            "xT": xT,
            "x_blk": np.ascontiguousarray(x2[cfg.BLK * c:cfg.BLK * (c + 1)]),
            "wqk": wqk_c,
            "wv": wv_c,
            "wout": w_out,
            "wgT": wgT,
            "wuT": wuT,
            "wdT": wdpT,
            "wg_sl": np.ascontiguousarray(
                w_gate[cfg.ISL * c:cfg.ISL * (c + 1)]),
            "wu_sl": np.ascontiguousarray(
                w_up[cfg.ISL * c:cfg.ISL * (c + 1)]),
            "wd_sl": np.ascontiguousarray(
                w_down[cfg.DSL * c:cfg.DSL * (c + 1)]),
            "gain2": np.ascontiguousarray(q_gain[[h0, h1]]).reshape(1, 2),
        })
    return in_maps


def kernel(L0_attn_x, L0_attn_w_qkv, L0_attn_w_out, L0_attn_q_gain,
           L0_ffn_w_gate, L0_ffn_w_up, L0_ffn_w_down,
           L0_ffn_up_reorder_index, L0_ffn_down_reorder_index):
    cfg = CFG()
    in_maps = make_in_maps(cfg, L0_attn_x, L0_attn_w_qkv, L0_attn_w_out,
                           L0_attn_q_gain, L0_ffn_w_gate, L0_ffn_w_up,
                           L0_ffn_w_down, L0_ffn_up_reorder_index,
                           L0_ffn_down_reorder_index)
    nc = build_program(cfg)
    res = run_bass_kernel_spmd(nc, in_maps, core_ids=list(range(cfg.NC)))
    blocks = [np.asarray(r["outT"]).T for r in res.results]
    return np.ascontiguousarray(
        np.concatenate(blocks, axis=0)[None]).astype(np.float32)
